# version 05 (final)
# baseline (speedup 1.0000x reference)
import numpy as np
import ml_dtypes
import concourse.bass as bass
import concourse.mybir as mybir
from concourse.bass import IndirectOffsetOnAxis
from concourse.bass_utils import run_bass_kernel_spmd

F32 = mybir.dt.float32
BF16 = mybir.dt.bfloat16
I32 = mybir.dt.int32
AF = mybir.ActivationFunctionType
ALU = mybir.AluOpType
AX = mybir.AxisListType

NCORES = 8
D = 1024
KC = 8
NT = 4096
NCH = 32
NSEQ = 2
T = 2048
NH = 16
HD = 64
EPS = 1e-5

ENGINES = ("sync", "scalar", "tensor", "vector", "gpsimd")
DEV_FLAGS = set()


class Prog:

    def __init__(self):
        self.q = {e: [] for e in ENGINES}
        self.semv = {}
        self.sem_handles = {}
        self.waited = {e: {} for e in ENGINES}

    def sem_names(self):
        return list(self.semv.keys())

    def _bump(self, sem, k):
        v = self.semv.get(sem, 0) + k
        self.semv[sem] = v
        return v

    SELF_SYNC = {"vector": "DVE", "scalar": "ACT", "gpsimd": "POOL"}

    def op(self, eng, fn, inc=None, k=1):
        auto = self.SELF_SYNC.get(eng)
        if auto is not None:
            self.wait(eng, self.token(auto))
            if inc is None or inc == auto:
                inc, k = auto, 1
            else:
                inc, k = auto, k
        if inc is not None:
            v = self._bump(inc, k)

            def wrapped(e, sem=inc, kk=k):
                r = fn(e)
                r.then_inc(self.sem_handles[sem], kk)
            self.q[eng].append(wrapped)
            return (inc, v)
        else:
            self.q[eng].append(lambda e: fn(e))
            return None

    def dma(self, eng, out, in_, sem=None, **kw):
        sem = sem or ("dS" if eng == "sync" else "dG" if eng == "gpsimd" else "dC")
        v = self._bump(sem, 16)

        def f(e):
            e.dma_start(out=out, in_=in_, **kw).then_inc(self.sem_handles[sem], 16)
        self.q[eng].append(f)
        return (sem, v)

    DMA_SEMS = {"dS": ("sync", "mS"), "dG": ("gpsimd", "mG"),
                "dC": ("scalar", "mC")}

    def wait(self, eng, tok):
        if tok is None:
            return
        sem, v = tok
        if sem in self.DMA_SEMS and eng != self.DMA_SEMS[sem][0]:
            owner, marker = self.DMA_SEMS[sem]
            key = (sem, v)
            if not hasattr(self, "_marks"):
                self._marks = {}
            if key not in self._marks:
                self.wait(owner, tok)
                mv = self._bump(marker, 1)

                def bump(e, marker=marker):
                    e.sem_inc(self.sem_handles[marker], 1)
                self.q[owner].append(bump)
                self._marks[key] = mv
            sem, v = marker, self._marks[key]
        prev = self.waited[eng].get(sem, -1)
        if v <= prev:
            return
        self.waited[eng][sem] = v
        self.q[eng].append(lambda e: e.wait_ge(self.sem_handles[sem], v))

    def waits(self, eng, toks):
        best = {}
        stack = list(toks)
        while stack:
            t = stack.pop()
            if t is None:
                continue
            if isinstance(t, list):
                stack.extend(t)
                continue
            s, v = t
            if v > best.get(s, -1):
                best[s] = v
        for s, v in best.items():
            self.wait(eng, (s, v))

    def token(self, sem):
        return (sem, self.semv.get(sem, 0))

    def barrier(self):
        toks = [self.token(s) for s in self.semv]
        for e in ENGINES:
            self.waits(e, toks)


class Rot:

    def __init__(self, tiles):
        self.tiles = tiles
        self.rel = [None] * len(tiles)
        self.i = 0

    def acquire(self):
        j = self.i % len(self.tiles)
        self.i += 1
        return j, self.tiles[j], self.rel[j]

    def done(self, j, tok):
        self.rel[j] = tok



def _prep_inputs(I):
    H = {}
    H["W_qkvT"] = np.ascontiguousarray(I["L0_attn_W_qkv"].T)
    H["b_qkv"] = I["L0_attn_b_qkv"].reshape(1, 3 * D)
    H["l0_Wr"] = I["L0_ffn_Wr"]
    H["l0_W1"] = I["L0_ffn_W1"]
    H["l0_W2"] = I["L0_ffn_W2"]

    H["l1_lam_raw"] = I["L1_attn_log_lambda_raw"].reshape(KC, 128, 1)
    H["l1_WbT"] = np.ascontiguousarray(I["L1_attn_Wb"].T)
    H["l1_bb"] = I["L1_attn_bb"].reshape(KC, 128, 1)
    H["l1_WcT"] = np.ascontiguousarray(I["L1_attn_Wc"].T)
    H["l1_bc"] = I["L1_attn_bc"].reshape(KC, 128, 1)
    H["l1_gamma"] = I["L1_attn_gamma"].reshape(1, D)
    H["l1_beta"] = I["L1_attn_beta"].reshape(1, D)
    H["l1_Wr"] = I["L1_ffn_router_W"]
    H["l1_br"] = I["L1_ffn_router_b"].reshape(1, 15)
    H["l1_W1"] = np.ascontiguousarray(
        np.concatenate([I["L1_ffn_eW1"], I["L1_ffn_sW1"][None]], 0))
    H["l1_W2"] = np.ascontiguousarray(
        np.concatenate([I["L1_ffn_eW2"], I["L1_ffn_sW2"][None]], 0))
    H["l1_b1"] = np.ascontiguousarray(np.concatenate(
        [I["L1_ffn_eb1"], I["L1_ffn_sb1"][None]], 0)).reshape(16, KC, 128, 1)
    H["l1_b2"] = np.ascontiguousarray(np.concatenate(
        [I["L1_ffn_eb2"], I["L1_ffn_sb2"][None]], 0)).reshape(16, 1, D)

    H["l2_WqT"] = np.ascontiguousarray(I["L2_attn_Wq"].T)
    H["l2_WkT"] = np.ascontiguousarray(I["L2_attn_Wk"].T)
    H["l2_WvT"] = np.ascontiguousarray(I["L2_attn_Wv"].T)
    H["l2_Wr"] = I["L2_ffn_Wr"]
    H["l2_br"] = I["L2_ffn_br"].reshape(1, 8)
    H["l2_W1"] = I["L2_ffn_W1"]
    H["l2_b1"] = I["L2_ffn_b1"].reshape(8, 32, 128, 1)
    H["l2_W2"] = I["L2_ffn_W2"]
    H["l2_b2"] = I["L2_ffn_b2"].reshape(8, 1, D)

    H["l3_W_attn"] = I["L3_attn_W_attn"]
    H["l3_b_attn"] = I["L3_attn_b_attn"].reshape(1, 3 * D)
    H["l3_W_proj"] = I["L3_attn_W_proj"]
    H["l3_b_proj"] = I["L3_attn_b_proj"].reshape(KC, 128, 1)
    H["l3_wgT"] = np.ascontiguousarray(I["L3_ffn_w_gate"].T)
    H["l3_wuT"] = np.ascontiguousarray(I["L3_ffn_w_up"].T)
    H["l3_wdT"] = np.ascontiguousarray(I["L3_ffn_w_down"].T)

    H["ident_bf"] = np.eye(128, dtype=ml_dtypes.bfloat16)
    H["ident_f32"] = np.eye(128, dtype=np.float32)
    H["ones_row_bf"] = np.ones((1, 128), dtype=ml_dtypes.bfloat16)
    H["ones_row_f32"] = np.ones((1, 128), dtype=np.float32)

    freqs = 1.0 / 10000.0 ** (np.arange(0, HD, 2, dtype=np.float64) / HD)
    ang = np.arange(T, dtype=np.float64)[:, None] * freqs[None, :]
    cos = np.cos(ang).astype(np.float32)
    sin = np.sin(ang).astype(np.float32)
    H["rope_cos"] = np.ascontiguousarray(np.tile(cos, (1, NH)))
    H["rope_sin"] = np.ascontiguousarray(np.tile(sin, (1, NH)))

    return H


def _l2_scale_tables():
    half = HD // 2
    theta = 1.0 / 10000.0 ** (np.arange(0, HD, 2, dtype=np.float64) / HD)
    pos = np.arange(16, dtype=np.float64)
    ang = pos[:, None] * theta[None, :]
    ang2 = np.concatenate([ang, ang], -1)
    c = np.cos(ang2)
    s = np.sin(ang2)
    sign = np.where(np.arange(HD) < half, 1.0, -1.0)
    m = (c + sign * s).astype(np.float32)
    return np.ascontiguousarray(np.tile(m, (1, NH)))



class Ctx:
    _uid = 0

    def mksb(self, stk, shape, dt):
        Ctx._uid += 1
        return stk.enter_context(self.nc.sbuf_tensor(f"t{Ctx._uid}", shape, dt))

    def mkpool(self, stk, n, shape, dt):
        return Rot([self.mksb(stk, shape, dt) for _ in range(n)])

    def mkps(self, stk, shape, dt):
        Ctx._uid += 1
        return stk.enter_context(self.nc.psum_tensor(f"q{Ctx._uid}", shape, dt))


def emit_ln_router(P, cx, blk):
    from contextlib import ExitStack
    nc = cx.nc
    r = blk.get("router")
    P.barrier()
    stk = blk["stk"]
    cx.xt_pool = cx.mkpool(stk, 2, [128, 1024], F32)
    cx.xsq_pool = cx.mkpool(stk, 2, [128, 1024], F32)
    cx.xnbf_pool = cx.mkpool(stk, 2, [128, 1024], BF16)
    cx.fmstage_pool = cx.mkpool(stk, 3, [128, 1024], BF16)
    cx.psB = cx.mkps(stk, [128, 1024], BF16)
    cx.psB_free = None
    cx.psD = cx.mkps(stk, [128, 512], F32)
    cx.psD2 = cx.psD[:, 128:256]
    cx.psD_free = None
    cx.psD2_free = None
    if r is not None:
        cx.gatesT = cx.mksb(stk, [16, 4096], F32)
        cx.gatesT_bf = cx.mksb(stk, [16, 4096], BF16)
        r["gatesT"] = cx.gatesT
        r["gatesT_bf"] = cx.gatesT_bf
    for c in range(NCH):
        jx, xt, war = cx.xt_pool.acquire()
        P.waits("sync", [war, blk.get("prev_tok")])
        t_x = P.dma("sync", xt[:], blk["x"][c * 128:(c + 1) * 128, :])

        js, stat, war_s = cx.stat_pool.acquire()
        jq, xsq, war_q = cx.xsq_pool.acquire()

        P.waits("vector", [t_x, war_s])
        t1 = P.op("vector", lambda e, xt=xt, stat=stat: e.reduce_sum(
            stat[:, 2:3], xt[:], axis=AX.X), inc="DVE")
        P.waits("scalar", [t_x, war_q])
        t2 = P.op("scalar", lambda e, xt=xt, xsq=xsq: e.activation(
            xsq[:], xt[:], AF.Square), inc="ACT")
        P.waits("vector", [t2])
        t3 = P.op("vector", lambda e, xsq=xsq, stat=stat: e.reduce_sum(
            stat[:, 3:4], xsq[:], axis=AX.X), inc="DVE")
        P.op("vector", lambda e, stat=stat: e.tensor_scalar(
            stat[:, 0:1], stat[:, 2:3], -1.0 / D, None, ALU.mult))
        P.op("vector", lambda e, stat=stat: e.tensor_tensor(
            stat[:, 4:5], stat[:, 0:1], stat[:, 0:1], ALU.mult))
        P.op("vector", lambda e, stat=stat: e.tensor_scalar(
            stat[:, 5:6], stat[:, 3:4], 1.0 / D, stat[:, 4:5], ALU.mult,
            ALU.subtract), inc="DVE")
        t4 = P.token("DVE")
        P.waits("scalar", [t4])
        t5 = P.op("scalar", lambda e, stat=stat: e.activation(
            stat[:, 6:7], stat[:, 5:6], AF.Sqrt, bias=cx.cst[:, 0:1]), inc="ACT")
        P.waits("vector", [t5])
        t6 = P.op("vector", lambda e, stat=stat: e.reciprocal(
            stat[:, 1:2], stat[:, 6:7]), inc="DVE")

        jn, xnbf, war_n = cx.xnbf_pool.acquire()
        P.waits("vector", [war_n])
        t7 = P.op("vector", lambda e, xt=xt, stat=stat, xnbf=xnbf: e.tensor_scalar(
            xnbf[:], xt[:], stat[:, 0:1], stat[:, 1:2], ALU.add, ALU.mult),
            inc="DVE")
        cx.xt_pool.done(jx, t7)
        cx.xsq_pool.done(jq, t3)
        cx.stat_pool.done(js, t7)

        jf, stage, war_f = cx.fmstage_pool.acquire()
        P.waits("tensor", [t7, cx.psB_free, war_f])
        for k in range(KC):
            tok_mm = P.op("tensor", lambda e, xnbf=xnbf, k=k: e.transpose(
                cx.psB[:, k * 128:(k + 1) * 128], xnbf[:, k * 128:(k + 1) * 128],
                cx.ident_bf[:]), inc="PE" if k == KC - 1 else None)
        P.waits("scalar", [tok_mm])
        t8 = P.op("scalar", lambda e, stage=stage: e.copy(
            out=stage[:], in_=cx.psB[:]), inc="ACT")
        cx.psB_free = t8
        cx.xnbf_pool.done(jn, tok_mm)

        P.waits("sync", [t8])
        dst = blk["xn"][:, :, c * 128:(c + 1) * 128].rearrange("k p f -> p k f")
        src = stage[:].rearrange("p (k f) -> p k f", k=KC)
        t9 = P.dma("sync", dst, src)
        tok_stage_consumed = [t9]

        if r is not None:
            Er = r["Er"]
            P.waits("tensor", [t8, cx.psD_free, cx.psD2_free])
            for k in range(KC):
                tok_lg = P.op("tensor", lambda e, stage=stage, k=k, Er=Er: e.matmul(
                    cx.psD[:, 0:Er], stage[:, k * 128:(k + 1) * 128],
                    r["wr_sb"][:, k * Er:(k + 1) * Er],
                    start=(k == 0), stop=(k == KC - 1)),
                    inc="PE" if k == KC - 1 else None)
            emit_router_tail(P, cx, r, c, tok_lg)
            tok_stage_consumed = [t9, P.token("PE")]
        cx.fmstage_pool.done(jf, tok_stage_consumed)


def emit_router_tail(P, cx, r, c, tok_lg):
    E = r["E"]
    Er = r.get("Er", E)
    kind = r["kind"]
    jp, pb, war_p = cx.probs_pool.acquire()
    P.waits("vector", [tok_lg, war_p])
    if r.get("br_bc") is not None:
        P.op("vector", lambda e, pb=pb: e.tensor_tensor(
            cx.psD[:, 0:Er], cx.psD[:, 0:Er], r["br_bc"][:, 0:Er], ALU.add))
    tmx = P.op("vector", lambda e, pb=pb: e.reduce_max(
        pb[:, 16:17], cx.psD[:, 0:Er], axis=AX.X, negate=True), inc="DVE")
    P.waits("scalar", [tmx])
    tex = P.op("scalar", lambda e, pb=pb: e.activation(
        pb[:, 0:Er], cx.psD[:, 0:Er], AF.Exp, bias=pb[:, 16:17],
        accum_out=pb[:, 17:18]), inc="ACT")
    cx.psD_free = tex
    P.waits("vector", [tex])
    P.op("vector", lambda e, pb=pb: e.reciprocal(pb[:, 18:19], pb[:, 17:18]))
    nsel = {"top2": 2, "top3": 3, "top1inv": 1}[kind]
    P.op("vector", lambda e, pb=pb: e.reduce_max(pb[:, 19:20], pb[:, 0:Er],
                                                 axis=AX.X))
    for kk in range(1, nsel):
        P.op("vector", lambda e, pb=pb: e.tensor_scalar(
            pb[:, 24:24 + Er], pb[:, 0:Er], pb[:, 19:20], None, ALU.is_ge))
        P.op("vector", lambda e, pb=pb: e.scalar_tensor_tensor(
            pb[:, 48:48 + Er], pb[:, 24:24 + Er], -1e30, pb[:, 0:Er],
            ALU.mult, ALU.add))
        P.op("vector", lambda e, pb=pb: e.reduce_max(
            pb[:, 19:20], pb[:, 48:48 + Er], axis=AX.X))
    P.op("vector", lambda e, pb=pb: e.tensor_scalar(
        pb[:, 24:24 + Er], pb[:, 0:Er], pb[:, 19:20], None, ALU.is_ge))
    if kind == "top1inv":
        tg = P.op("vector", lambda e, pb=pb: e.tensor_scalar(
            pb[:, 80:80 + Er], pb[:, 24:24 + Er], pb[:, 18:19], None, ALU.mult),
            inc="DVE")
    else:
        P.op("vector", lambda e, pb=pb: e.tensor_scalar(
            pb[:, 48:48 + Er], pb[:, 0:Er], pb[:, 18:19], None, ALU.mult))
        tg = P.op("vector", lambda e, pb=pb: e.tensor_tensor(
            pb[:, 80:80 + Er], pb[:, 48:48 + Er], pb[:, 24:24 + Er], ALU.mult),
            inc="DVE")
    if E > Er:
        tg = P.op("vector", lambda e, pb=pb: e.memset(pb[:, 80 + Er:80 + E], 1.0),
                  inc="DVE")
    P.waits("tensor", [tg, cx.psD2_free])
    tt = P.op("tensor", lambda e, pb=pb, E=E: e.transpose(
        cx.psD2[0:E, 0:128], pb[:, 80:80 + E], cx.ident_f32[:, 0:128]), inc="PE")
    P.waits("vector", [tt])
    P.op("vector", lambda e, c=c, E=E: e.tensor_copy(
        r["gatesT"][0:E, c * 128:(c + 1) * 128], cx.psD2[0:E, 0:128]))
    t2 = P.op("vector", lambda e, c=c, E=E: e.tensor_copy(
        r["gatesT_bf"][0:E, c * 128:(c + 1) * 128], cx.psD2[0:E, 0:128]),
        inc="DVE")
    cx.psD2_free = t2
    if getattr(cx, "dbg_router", None) is not None and c == 0:
        P.waits("sync", [t2])
        P.dma("sync", cx.dbg_router, pb[:])
    cx.probs_pool.done(jp, t2)


def emit_moe_dense(P, cx, blk):
    from contextlib import ExitStack
    nc = cx.nc
    E, HID = blk["E"], blk["HID"]
    HC = HID // 128
    Hbuf = blk["Hbuf"]
    P.barrier()
    with ExitStack() as stk:
        cx.psD = cx.mkps(stk, [128, 512], F32)
        cx.psD_free = None
        w1_pool = cx.mkpool(stk, 1, [128, KC * HID], BF16)
        w2_pool = cx.mkpool(stk, 1, [128, HC * 512], BF16)
        hnt_pool = cx.mkpool(stk, 1, [128, HC * 512], BF16)
        hg_pool = cx.mkpool(stk, 2, [128, HC * 256], BF16)
        xnr_pool = cx.mkpool(stk, 2, [128, KC * 512], BF16)
        gate_pool = cx.mkpool(stk, 2, [128, 512], F32)
        st1_pool = cx.mkpool(stk, 3, [128, 512], F32)
        fst_pool = cx.mkpool(stk, 3, [128, 512], F32)
        b1_pool = cx.mkpool(stk, 2, [128, 32], F32)
        b2_pool = cx.mkpool(stk, 2, [1, 1024], BF16)
        gbf_pool = cx.mkpool(stk, 1, [1, NT], BF16)

        ftok = {}
        for e in range(E):
            jgb, growbf, war_gb = gbf_pool.acquire()
            P.waits("sync", [war_gb])
            t_gb = P.dma("sync", growbf[0:1, :], cx.GTBF[e:e + 1, :])
            jw1, w1, war_w1 = w1_pool.acquire()
            P.waits("gpsimd", [war_w1])
            w1v = w1.rearrange("p (k h) -> p k h", k=KC)
            t_w1 = P.dma("gpsimd", w1v,
                         blk["W1"][e].rearrange("(k p) h -> p k h", p=128))
            t_b1 = None
            if blk["b1"] is not None:
                jb1, b1t, war_b1 = b1_pool.acquire()
                P.waits("gpsimd", [war_b1])
                t_b1 = P.dma("gpsimd", b1t[:, 0:HC],
                             blk["b1"][e].rearrange("h p one -> p (h one)"),
                             allow_slow_non_contiguous=True)
            tok_hnt = {}
            for nt in range(NT // 512):
                jx, xn_rhs, war_x = xnr_pool.acquire()
                xv = xn_rhs.rearrange("p (k f) -> p k f", k=KC)
                P.waits("sync", [war_x])
                t_xn = P.dma(
                    "sync", xv,
                    cx.XN[:, :, nt * 512:(nt + 1) * 512].rearrange(
                        "k p f -> p k f"))
                jg, gate_sb, war_g = gate_pool.acquire()
                P.waits("tensor", [cx.psD_free, t_gb])
                t_gm = P.op("tensor", lambda e2, growbf=growbf, nt=nt: e2.matmul(
                    cx.psD[:, 0:512], cx.ones_bf[0:1, :],
                    growbf[0:1, nt * 512:(nt + 1) * 512],
                    start=True, stop=True), inc="PE")
                P.waits("vector", [t_gm, war_g])
                t_gv = P.op("vector", lambda e2, gate_sb=gate_sb: e2.tensor_copy(
                    gate_sb[:], cx.psD[:, 0:512]))
                cx.psD_free = t_gv
                jh, hnt, war_h = hnt_pool.acquire()
                last_dve = None
                for hc in range(HC):
                    pst = cx.psC0 if hc % 2 == 0 else cx.psC1
                    pfree = cx.psC0_free if hc % 2 == 0 else cx.psC1_free
                    P.waits("tensor", [t_w1, t_xn, pfree])
                    for k in range(KC):
                        t_mm = P.op(
                            "tensor",
                            lambda e2, k=k, hc=hc, pst=pst, w1v=w1v, xv=xv:
                            e2.matmul(pst[:],
                                      w1v[:, k, hc * 128:(hc + 1) * 128],
                                      xv[:, k, :],
                                      start=(k == 0), stop=(k == KC - 1)),
                            inc="PE" if k == KC - 1 else None)
                    js, st1, war_s = st1_pool.acquire()
                    P.waits("scalar", [t_mm, war_s] + ([t_b1] if t_b1 else []))
                    bias = b1t[:, hc:hc + 1] if t_b1 else cx.cst[:, 1:2]
                    t_r = P.op("scalar",
                               lambda e2, st1=st1, pst=pst, bias=bias:
                               e2.activation(st1[:], pst[:], AF.Relu, bias=bias),
                               inc="ACT")
                    if hc % 2 == 0:
                        cx.psC0_free = t_r
                    else:
                        cx.psC1_free = t_r
                    P.waits("vector", [t_r] + ([war_h] if hc == 0 else []))
                    last_dve = P.op(
                        "vector",
                        lambda e2, hnt=hnt, hc=hc, st1=st1, gate_sb=gate_sb:
                        e2.tensor_mul(hnt[:, hc * 512:(hc + 1) * 512], st1[:],
                                      gate_sb[:]), inc="DVE")
                    st1_pool.done(js, last_dve)
                xnr_pool.done(jx, P.token("PE"))
                gate_pool.done(jg, last_dve)
                P.waits("sync", [last_dve])
                t_hst = P.dma(
                    "sync",
                    Hbuf[e, :, :, nt * 512:(nt + 1) * 512].rearrange(
                        "h p f -> p h f"),
                    hnt.rearrange("p (h f) -> p h f", h=HC))
                tok_hnt[nt] = t_hst
                hnt_pool.done(jh, t_hst)
            if blk["b1"] is not None:
                b1_pool.done(jb1, P.token("ACT"))
            w1_pool.done(jw1, P.token("PE"))

            t_b2 = None
            if blk["b2"] is not None:
                jb2, b2t, war_b2 = b2_pool.acquire()
                P.waits("gpsimd", [war_b2])
                t_b2 = P.dma("gpsimd", b2t[0:1, :], blk["b2"][e])
            for half in range(2):
                jw2, w2, war_w2 = w2_pool.acquire()
                P.waits("gpsimd", [war_w2])
                w2v = w2.rearrange("p (h f) -> p h f", h=HC)
                t_w2 = P.dma(
                    "gpsimd", w2v,
                    blk["W2"][e][:, half * 512:(half + 1) * 512].rearrange(
                        "(h p) f -> p h f", p=128))
                for g in range(NT // 256):
                    jh2, hg, war_h2 = hg_pool.acquire()
                    hgv = hg.rearrange("p (h f) -> p h f", h=HC)
                    P.waits("sync", [war_h2, tok_hnt[g // 2] if
                             "h_wait_all" not in DEV_FLAGS else
                             tok_hnt[max(tok_hnt)]])
                    t_hg = P.dma(
                        "sync", hgv,
                        Hbuf[e, :, :, g * 256:(g + 1) * 256].rearrange(
                            "h p f -> p h f"))
                    for tc in range(2):
                        pq = (g * 2 + tc) % 4
                        pslice = cx.psA[:, pq * 512:(pq + 1) * 512]
                        pfree = cx.psA_frees[pq]
                        P.waits("tensor", [t_hg, t_w2, pfree, t_gb] +
                                ([t_b2] if t_b2 is not None else []))
                        nmm = HC + (1 if blk["b2"] is not None else 0)
                        for hc in range(HC):
                            t_mm2 = P.op(
                                "tensor",
                                lambda e2, hc=hc, tc=tc, pslice=pslice,
                                hgv=hgv, w2v=w2v:
                                e2.matmul(
                                    pslice[:],
                                    hgv[:, hc, tc * 128:(tc + 1) * 128],
                                    w2v[:, hc, :],
                                    start=(hc == 0),
                                    stop=(blk["b2"] is None and hc == HC - 1)),
                                inc="PE" if hc == nmm - 1 else None)
                        if blk["b2"] is not None:
                            t_mm2 = P.op(
                                "tensor",
                                lambda e2, g=g, tc=tc, half=half,
                                pslice=pslice, b2t=b2t, growbf=growbf:
                                e2.matmul(
                                    pslice[:],
                                    growbf[0:1, g * 256 + tc * 128:
                                           g * 256 + (tc + 1) * 128],
                                    b2t[0:1, half * 512:(half + 1) * 512],
                                    start=False, stop=True),
                                inc="PE")
                        jf, fst, war_f = fst_pool.acquire()
                        P.waits("vector", [t_mm2, war_f])
                        t_ev = P.op("vector",
                                    lambda e2, fst=fst, pslice=pslice:
                                    e2.tensor_copy(fst[:], pslice[:]),
                                    inc="DVE")
                        cx.psA_frees[pq] = t_ev
                        row0 = g * 256 + tc * 128
                        key = (g, tc, half)
                        P.waits("gpsimd", [t_ev] +
                                ([ftok[key]] if key in ftok else []))
                        first = (e == 0 and not blk.get("accum_first_add"))
                        t_acc = P.dma(
                            "gpsimd",
                            cx.DTM[row0:row0 + 128,
                                   half * 512:(half + 1) * 512], fst[:],
                            accum_op=(ALU.bypass if first else ALU.add))
                        ftok[key] = t_acc
                        fst_pool.done(jf, t_acc)
                    hg_pool.done(jh2, P.token("PE"))
                w2_pool.done(jw2, P.token("PE"))
            gbf_pool.done(jgb, P.token("PE"))
            if blk["b2"] is not None:
                b2_pool.done(jb2, P.token("PE"))
    return list(ftok.values())


def emit_attn(P, cx, blk):
    from contextlib import ExitStack
    kind = blk["kind"]
    din = blk["din"]
    P.barrier()
    with ExitStack() as stk:
        psBT = cx.mkps(stk, [128, 2048], BF16)
        psBT_free = None
        wq = cx.mksb(stk, [128, KC * 3072], BF16)
        wqv = wq.rearrange("p (k j) -> p k j", k=KC)
        QF = cx.mksb(stk, [128, 8 * 2048], BF16)
        KF = cx.mksb(stk, [128, 8 * 2048], BF16)
        Vt = cx.mksb(stk, [128, 8 * 2048], BF16)
        QFv = QF.rearrange("p (pr t) -> p pr t", pr=8)
        KFv = KF.rearrange("p (pr t) -> p pr t", pr=8)
        Vtv = Vt.rearrange("p (pr c x) -> p pr c x", pr=8, c=16)
        has_bias = blk.get("brow") is not None
        if has_bias:
            brow = cx.mksb(stk, [1, 3072], BF16)

        if kind == "l2":
            toks_w = []
            for ji, wn in enumerate(("l2_WqT", "l2_WkT", "l2_WvT")):
                toks_w.append(P.dma(
                    "gpsimd", wqv[:, :, ji * 1024:(ji + 1) * 1024],
                    din[wn].rearrange("(k p) j -> p k j", p=128)))
            t_w = toks_w[-1]
        else:
            wname = "W_qkvT" if kind == "l0" else "l3_W_attn"
            t_w = P.dma("gpsimd", wqv,
                        din[wname].rearrange("(k p) j -> p k j", p=128))
        t_br = None
        if has_bias:
            t_br = P.dma("gpsimd", brow[0:1, :], blk["brow"])
        for s in range(NSEQ):
            P.barrier()
            stkA = ExitStack()
            qt_pool = cx.mkpool(stkA, 1, [128, 3072], F32)
            xnc_pool = cx.mkpool(stkA, 2, [128, 1024], BF16)
            qkbf_pool = cx.mkpool(stkA, 2, [128, 2048], BF16)
            if kind == "l0":
                rtab_pool = cx.mkpool(stkA, 1, [128, 1024], F32)
                rsc = cx.mksb(stkA, [128, 1024], F32)
            if kind == "l2":
                m2_sb = cx.mksb(stkA, [128, 1024], F32)
                t_m2 = P.dma("sync", m2_sb[:], blk["m2tab"][s])
            for c in range(16):
                gc = s * 16 + c
                jx, xnc, war_x = xnc_pool.acquire()
                P.waits("sync", [war_x])
                t_xn = P.dma("sync", xnc.rearrange("p (k f) -> p k f", k=KC),
                             cx.XN[:, :, gc * 128:(gc + 1) * 128].rearrange(
                                 "k p f -> p k f"))
                xncv = xnc.rearrange("p (k f) -> p k f", k=KC)
                jq, qt, war_q = qt_pool.acquire()
                for j in range(6):
                    pst = [cx.psA[:, 0:512], cx.psA[:, 512:1024],
                           cx.psA[:, 1024:1536], cx.psA[:, 1536:2048],
                           cx.psC0, cx.psC1][j]
                    pfree = (cx.psA_frees[j] if j < 4 else
                             (cx.psC0_free if j == 4 else cx.psC1_free))
                    P.waits("tensor", [t_w, t_xn, pfree] +
                            ([t_br] if has_bias else []))
                    for k in range(KC):
                        t_mm = P.op("tensor",
                                    lambda e, k=k, j=j, pst=pst, xncv=xncv:
                                    e.matmul(pst[:],
                                             xncv[:, k, :],
                                             wqv[:, k, j * 512:(j + 1) * 512],
                                             start=(k == 0),
                                             stop=(k == KC - 1 and
                                                   not has_bias)),
                                    inc="PE" if (k == KC - 1 and
                                                 not has_bias) else None)
                    if has_bias:
                        t_mm = P.op("tensor", lambda e, j=j, pst=pst:
                                    e.matmul(pst[:], cx.ones_bf[0:1, :],
                                             brow[0:1, j * 512:(j + 1) * 512],
                                             start=False, stop=True), inc="PE")
                    P.waits("vector", [t_mm] + ([war_q] if j == 0 else []))
                    t_ev = P.op("vector", lambda e, j=j, qt=qt, pst=pst:
                                e.tensor_copy(qt[:, j * 512:(j + 1) * 512],
                                              pst[:]), inc="DVE")
                    if j < 4:
                        cx.psA_frees[j] = t_ev
                    elif j == 4:
                        cx.psC0_free = t_ev
                    else:
                        cx.psC1_free = t_ev
                jb, qkbf, war_b = qkbf_pool.acquire()
                P.waits("vector", [war_b])
                if kind == "l0" and "no_rope" in DEV_FLAGS:
                    P.op("vector", lambda e, qt=qt, qkbf=qkbf: e.tensor_copy(
                        qkbf[:, 0:2048], qt[:, 0:2048]))
                elif kind == "l0":
                    jr, rtab, war_r = rtab_pool.acquire()
                    P.waits("sync", [war_r])
                    t_ct = P.dma("sync", rtab[:, 0:512],
                                 din["rope_cos"][c * 128:(c + 1) * 128, :])
                    t_st = P.dma("sync", rtab[:, 512:1024],
                                 din["rope_sin"][c * 128:(c + 1) * 128, :])
                    tlast = None
                    for off in (0, 1024):
                        ev = qt[:, off:off + 1024].rearrange(
                            "p (x two) -> p x two", two=2)[:, :, 0:1]
                        ov = qt[:, off:off + 1024].rearrange(
                            "p (x two) -> p x two", two=2)[:, :, 1:2]
                        bev = qkbf[:, off:off + 1024].rearrange(
                            "p (x two) -> p x two", two=2)[:, :, 0:1]
                        bov = qkbf[:, off:off + 1024].rearrange(
                            "p (x two) -> p x two", two=2)[:, :, 1:2]
                        ct = rtab[:, 0:512].rearrange("p (x one) -> p x one", one=1)
                        st2 = rtab[:, 512:1024].rearrange("p (x one) -> p x one", one=1)
                        r1 = rsc[:, 0:512].rearrange("p (x one) -> p x one", one=1)
                        r3 = rsc[:, 512:1024].rearrange("p (x one) -> p x one", one=1)
                        P.waits("vector", [t_ct, t_st])
                        P.op("vector", lambda e, ev=ev, ct=ct, r1=r1:
                             e.tensor_mul(r1, ev, ct))
                        P.op("vector", lambda e, ov=ov, st2=st2, r3=r3:
                             e.tensor_mul(r3, ov, st2))
                        P.op("vector", lambda e, bev=bev, r1=r1, r3=r3:
                             e.tensor_tensor(bev, r1, r3, ALU.subtract))
                        P.op("vector", lambda e, ov=ov, ct=ct, r1=r1:
                             e.tensor_mul(r1, ov, ct))
                        P.op("vector", lambda e, ev=ev, st2=st2, r3=r3:
                             e.tensor_mul(r3, ev, st2))
                        tlast = P.op("vector", lambda e, bov=bov, r1=r1, r3=r3:
                                     e.tensor_tensor(bov, r1, r3, ALU.add),
                                     inc="DVE")
                    rtab_pool.done(jr, tlast)
                elif kind == "l2":
                    P.waits("vector", [t_m2])
                    P.op("vector", lambda e, qt=qt, qkbf=qkbf: e.tensor_mul(
                        qkbf[:, 0:1024], qt[:, 0:1024], m2_sb[:]))
                    P.op("vector", lambda e, qt=qt, qkbf=qkbf: e.tensor_mul(
                        qkbf[:, 1024:2048], qt[:, 1024:2048], m2_sb[:]))
                else:
                    P.op("vector", lambda e, qt=qt, qkbf=qkbf: e.tensor_copy(
                        qkbf[:, 0:2048], qt[:, 0:2048]))
                if "no_v" in DEV_FLAGS:
                    t_v = P.op("vector", lambda e: e.engine_nop(), inc="DVE")
                else:
                    t_v = P.op("vector", lambda e, qt=qt, c=c: e.tensor_copy(
                        Vtv[:, :, c, :],
                        qt[:, 2048:3072].rearrange("p (pr x) -> p pr x", pr=8)),
                        inc="DVE")
                qt_pool.done(jq, t_v)
                if "no_tr" in DEV_FLAGS:
                    qkbf_pool.done(jb, t_v)
                    xnc_pool.done(jx, P.token("PE"))
                    continue
                P.waits("tensor", [t_v, psBT_free])
                for pr in range(8):
                    for half in range(2):
                        h = 2 * pr + half
                        tp = (0, 64) if half else None
                        P.op("tensor", lambda e, h=h, pr=pr, half=half,
                             qkbf=qkbf, tp=tp: e.transpose(
                                 psBT[half * 64:half * 64 + 64,
                                      pr * 128:(pr + 1) * 128],
                                 qkbf[:, h * 64:(h + 1) * 64],
                                 cx.ident_bf[:], tile_position=tp))
                        t_tr = P.op("tensor", lambda e, h=h, pr=pr, half=half,
                                    qkbf=qkbf, tp=tp: e.transpose(
                                        psBT[half * 64:half * 64 + 64,
                                             1024 + pr * 128:1024 + (pr + 1) * 128],
                                        qkbf[:, 1024 + h * 64:1024 + (h + 1) * 64],
                                        cx.ident_bf[:], tile_position=tp),
                                    inc="PE" if (pr == 7 and half == 1) else None)
                P.waits("vector", [t_tr])
                P.op("vector", lambda e, c=c: e.tensor_copy(
                    QFv[:, :, c * 128:(c + 1) * 128],
                    psBT[:, 0:1024].rearrange("p (pr x) -> p pr x", pr=8)))
                t_qf = P.op("vector", lambda e, c=c: e.tensor_copy(
                    KFv[:, :, c * 128:(c + 1) * 128],
                    psBT[:, 1024:2048].rearrange("p (pr x) -> p pr x", pr=8)),
                    inc="DVE")
                psBT_free = t_qf
                qkbf_pool.done(jb, t_tr)
                xnc_pool.done(jx, P.token("PE"))

            stkA.close()
            if "no_phaseB" in DEV_FLAGS:
                continue
            P.barrier()
            stkB = ExitStack()
            P_pool = cx.mkpool(stkB, 2, [128, 2048], BF16)
            PT_pool = cx.mkpool(stkB, 2 if kind != "l3" else 1,
                                [128, 16 * 512], BF16)
            ao_pool = cx.mkpool(stkB, 3, [128, 512],
                                BF16 if kind == "l3" else F32)
            if kind == "l3":
                rel0 = cx.mksb(stkB, [128, 2048], F32)
                cmask4 = cx.mksb(stkB, [128, 4 * 512], F32)
                zt = cx.mksb(stkB, [128, 512], F32)
                t_z = P.op("gpsimd", lambda e: e.memset(zt[:], 0.0), inc="POOL")
                t_rel = P.op("gpsimd", lambda e, rel0=rel0: e.iota(
                    rel0[:], [[1, 2048]], base=0, channel_multiplier=-1,
                    allow_small_or_imprecise_dtypes=True), inc="POOL")
                for qcin in range(4):
                    t_rel = P.op("gpsimd", lambda e, qcin=qcin: e.affine_select(
                        cmask4[:, qcin * 512:(qcin + 1) * 512], zt[:],
                        [[-1, 512]], ALU.is_ge, -1e30, base=qcin * 128,
                        channel_multiplier=1), inc="POOL")
            pvctr = [0]
            pending = [None]

            def emit_pv(p):
                pq = pvctr[0] % 2
                pvctr[0] += 1
                pv_ps = cx.psC0 if pq == 0 else cx.psC1
                pv_free = cx.psC0_free if pq == 0 else cx.psC1_free
                P.waits("tensor", [p["t_pt"], pv_free])
                for kc in range(p["kcmax"]):
                    t_pv = P.op(
                        "tensor",
                        lambda e, kc=kc, p=p, pv_ps=pv_ps:
                        e.matmul(pv_ps[0:64, :],
                                 Vtv[:, p["pr"], kc,
                                     p["half"] * 64:p["half"] * 64 + 64],
                                 p["PT"][:, kc * 512:(kc + 1) * 512],
                                 start=(kc == 0), stop=(kc == p["kcmax"] - 1)),
                        inc="PE" if kc == p["kcmax"] - 1 else None)
                PT_pool.done(p["jpt"], t_pv)
                ja, ao, war_a = ao_pool.acquire()
                P.waits("vector", [t_pv, war_a])
                t_ao = P.op("vector", lambda e, ao=ao, pv_ps=pv_ps:
                            e.tensor_copy(ao[0:64, :], pv_ps[0:64, :]),
                            inc="DVE")
                if pq == 0:
                    cx.psC0_free = t_ao
                else:
                    cx.psC1_free = t_ao
                P.waits("sync", [t_ao])
                if kind == "l3":
                    t_st = P.dma("sync",
                                 blk["AO"][p["h"] // 2,
                                           (p["h"] % 2) * 64:(p["h"] % 2) * 64 + 64,
                                           p["col0"]:p["col0"] + 512],
                                 ao[0:64, :])
                else:
                    t_st = P.dma("sync",
                                 cx.DFM[p["h"] // 2,
                                        (p["h"] % 2) * 64:(p["h"] % 2) * 64 + 64,
                                        p["col0"]:p["col0"] + 512], ao[0:64, :])
                ao_pool.done(ja, t_st)

            for h in range(NH):
                pr, half = h // 2, h % 2
                hb0, hb1 = half * 64, half * 64 + 64
                for qg in range(4):
                    jpt, PT, war_pt = PT_pool.acquire()
                    for qcin in range(4):
                        qc = qg * 4 + qcin
                        if kind == "l3":
                            W512 = ((qc * 128 + 128 + 511) // 512) * 512
                            kcmax_g = min(qg * 4 + 4, 16)
                        else:
                            W512 = 2048
                            kcmax_g = 16
                        nj = W512 // 512
                        js, stt, war_s = cx.stat_pool.acquire()
                        P.waits("tensor", cx.psA_frees[0:nj])
                        P.waits("vector", [war_s])
                        for j in range(nj):
                            t_sc = P.op(
                                "tensor",
                                lambda e, j=j, qc=qc, pr=pr, hb0=hb0, hb1=hb1:
                                e.matmul(cx.psA[:, j * 512:(j + 1) * 512],
                                         QFv[:, pr,
                                             qc * 128:(qc + 1) * 128][hb0:hb1],
                                         KFv[:, pr,
                                             j * 512:(j + 1) * 512][hb0:hb1],
                                         start=True, stop=True),
                                inc="PE" if (j == nj - 1 or (
                                    kind != "l3" and
                                    "no_partial_red" not in DEV_FLAGS))
                                else None)
                            if kind != "l3" and "no_partial_red" not in DEV_FLAGS:
                                P.waits("vector", [t_sc])
                                P.op("vector", lambda e, stt=stt, j=j:
                                     e.reduce_max(stt[:, 8 + j:9 + j],
                                                  cx.psA[:, j * 512:
                                                         (j + 1) * 512],
                                                  axis=AX.X))
                        if qcin == 0 and pending[0] is not None:
                            emit_pv(pending[0])
                            pending[0] = None
                        if kind == "l3":
                            slope = float(2.0 ** (-(h + 1)))
                            P.waits("vector", [t_sc, t_rel, war_s])
                            P.op("vector", lambda e, slope=slope, W512=W512:
                                 e.scalar_tensor_tensor(
                                     cx.psA[:, 0:W512], rel0[:, 0:W512], slope,
                                     cx.psA[:, 0:W512], ALU.mult, ALU.add))
                            qcin_ = qc % 4
                            P.op("vector", lambda e, W512=W512, qcin_=qcin_:
                                 e.tensor_tensor(
                                     cx.psA[:, W512 - 512:W512],
                                     cx.psA[:, W512 - 512:W512],
                                     cmask4[:, qcin_ * 512:(qcin_ + 1) * 512],
                                     ALU.add))
                            P.op("vector", lambda e, stt=stt, W512=W512:
                                 e.reduce_max(stt[:, 0:1], cx.psA[:, 0:W512],
                                              axis=AX.X, negate=True))
                        elif "no_partial_red" not in DEV_FLAGS:
                            P.op("vector", lambda e, stt=stt, nj=nj:
                                 e.reduce_max(stt[:, 0:1], stt[:, 8:8 + nj],
                                              axis=AX.X, negate=True))
                        else:
                            P.waits("vector", [t_sc])
                            P.op("vector", lambda e, stt=stt, W512=W512:
                                 e.reduce_max(stt[:, 0:1], cx.psA[:, 0:W512],
                                              axis=AX.X, negate=True))
                        t_mx = P.op("vector", lambda e, stt=stt:
                                    e.tensor_scalar(stt[:, 3:4], stt[:, 0:1],
                                                    0.125, None, ALU.mult),
                                    inc="DVE")
                        jp, Pt, war_p = P_pool.acquire()
                        P.waits("scalar", [t_mx, war_p])
                        t_ex = P.op("scalar", lambda e, Pt=Pt, stt=stt, W512=W512:
                                    e.activation(Pt[:, 0:W512],
                                                 cx.psA[:, 0:W512], AF.Exp,
                                                 bias=stt[:, 3:4], scale=0.125,
                                                 accum_out=stt[:, 1:2]),
                                    inc="ACT")
                        for j in range(nj):
                            cx.psA_frees[j] = t_ex
                        P.waits("vector", [t_ex])
                        P.op("vector", lambda e, stt=stt: e.reciprocal(
                            stt[:, 2:3], stt[:, 1:2]))
                        t_pn = P.op("vector", lambda e, Pt=Pt, stt=stt, W512=W512:
                                    e.tensor_scalar(Pt[:, 0:W512],
                                                    Pt[:, 0:W512],
                                                    stt[:, 2:3], None,
                                                    ALU.mult), inc="DVE")
                        cx.stat_pool.done(js, t_pn)
                        if kind == "l3" and W512 < kcmax_g * 128:
                            t_pn = P.op("vector",
                                        lambda e, Pt=Pt, W512=W512, kcmax_g=kcmax_g:
                                        e.memset(Pt[:, W512:kcmax_g * 128], 0.0),
                                        inc="DVE")
                        P.waits("tensor", [t_pn, psBT_free] +
                                ([war_pt] if qcin == 0 else []))
                        for kc in range(kcmax_g):
                            t_tr = P.op("tensor", lambda e, kc=kc, Pt=Pt:
                                        e.transpose(
                                            psBT[:, kc * 128:(kc + 1) * 128],
                                            Pt[:, kc * 128:(kc + 1) * 128],
                                            cx.ident_bf[:]),
                                        inc="PE" if kc == kcmax_g - 1 else None)
                        P_pool.done(jp, t_tr)
                        P.waits("vector", [t_tr])
                        t_pt = P.op("vector", lambda e, qcin=qcin, kcmax_g=kcmax_g, PT=PT:
                                    e.tensor_copy(
                                        PT.rearrange("p (kc q) -> p kc q",
                                                     kc=16)[:, 0:kcmax_g,
                                                            qcin * 128:
                                                            (qcin + 1) * 128],
                                        psBT[:, 0:kcmax_g * 128].rearrange(
                                            "p (kc x) -> p kc x", kc=kcmax_g)),
                                    inc="DVE")
                        psBT_free = t_pt
                    pend = dict(PT=PT, jpt=jpt, t_pt=t_pt, kcmax=kcmax_g,
                                pr=pr, half=half, h=h,
                                col0=s * 2048 + qg * 512)
                    if kind != "l3" and "no_delay_pv" not in DEV_FLAGS:
                        pending[0] = pend
                    else:
                        emit_pv(pend)
            if pending[0] is not None:
                emit_pv(pending[0])
                pending[0] = None
            P.barrier()
            stkB.close()

        if kind == "l3":
            P.barrier()
            wp = cx.mksb(stk, [128, KC * 1024], BF16)
            wpv = wp.rearrange("p (k j) -> p k j", k=KC)
            bp = cx.mksb(stk, [128, KC], F32)
            aor_pool = cx.mkpool(stk, 2, [128, KC * 512], BF16)
            prj_pool = cx.mkpool(stk, 3, [128, 512], F32)
            t_wp = P.dma("gpsimd", wpv,
                         din["l3_W_proj"].rearrange("(k p) j -> p k j", p=128))
            t_bp = P.dma("gpsimd", bp[:, :],
                         din["l3_b_proj"].rearrange("k p one -> p (k one)"),
                         allow_slow_non_contiguous=True)
            for tt in range(8):
                ja, aor, war_a = aor_pool.acquire()
                P.waits("sync", [war_a])
                t_ar = P.dma("sync",
                             aor.rearrange("p (k f) -> p k f", k=KC),
                             blk["AO"][:, :, tt * 512:(tt + 1) * 512].rearrange(
                                 "k p f -> p k f"))
                aorv = aor.rearrange("p (k f) -> p k f", k=KC)
                for oc in range(KC):
                    pst = cx.psC0 if oc % 2 == 0 else cx.psC1
                    pfree = cx.psC0_free if oc % 2 == 0 else cx.psC1_free
                    P.waits("tensor", [t_wp, t_ar, pfree])
                    for k in range(KC):
                        t_mm = P.op("tensor",
                                    lambda e, k=k, oc=oc, pst=pst, aorv=aorv:
                                    e.matmul(pst[:],
                                             wpv[:, k, oc * 128:(oc + 1) * 128],
                                             aorv[:, k, :],
                                             start=(k == 0), stop=(k == KC - 1)),
                                    inc="PE" if k == KC - 1 else None)
                    jj, prj, war_j = prj_pool.acquire()
                    P.waits("vector", [t_mm, war_j, t_bp])
                    t_pj = P.op("vector", lambda e, prj=prj, pst=pst, oc=oc:
                                e.tensor_scalar(prj[:], pst[:],
                                                bp[:, oc:oc + 1], None,
                                                ALU.add), inc="DVE")
                    if oc % 2 == 0:
                        cx.psC0_free = t_pj
                    else:
                        cx.psC1_free = t_pj
                    P.waits("sync", [t_pj])
                    t_st = P.dma("sync",
                                 cx.DFM[oc, :, tt * 512:(tt + 1) * 512], prj[:])
                    prj_pool.done(jj, t_st)
                aor_pool.done(ja, P.token("PE"))
    return [P.token("dS")]


def emit_l1_mixer(P, cx, blk):
    from contextlib import ExitStack
    nc = cx.nc
    din = blk["din"]
    P.barrier()
    toks = []
    with ExitStack() as stk:
        w_pool = cx.mkpool(stk, 2, [128, KC * 1024], BF16)
        HH = cx.mksb(stk, [128, KC * NT], BF16)
        HHv = HH.rearrange("p (k t) -> p k t", k=KC)
        vstage_pool = cx.mkpool(stk, 1, [128, NT], F32)
        xnr_pool = cx.mkpool(stk, 2, [128, KC * 512], BF16)
        lam_sb = cx.mksb(stk, [128, 32], F32)
        lam_bc = cx.mksb(stk, [128, 2048], F32)
        gb_sb = cx.mksb(stk, [128, 2048], F32)
        grow = cx.mksb(stk, [1, 2048], F32)
        zsq = cx.mksb(stk, [128, 1024], F32)
        zn_pool = cx.mkpool(stk, 2, [128, 1024], F32)
        xnb_pool = cx.mkpool(stk, 2, [128, 512], BF16)
        cx.psD = cx.mkps(stk, [128, 512], F32)
        cx.psD_free = None

        t_lr = P.dma("sync", lam_sb[:, 0:KC],
                     din["l1_lam_raw"].rearrange("k p one -> p (k one)"),
                     allow_slow_non_contiguous=True)
        t_bb = P.dma("sync", lam_sb[:, 16:16 + KC],
                     din["l1_bb"].rearrange("k p one -> p (k one)"),
                     allow_slow_non_contiguous=True)
        t_bc = P.dma("sync", lam_sb[:, 24:24 + KC],
                     din["l1_bc"].rearrange("k p one -> p (k one)"),
                     allow_slow_non_contiguous=True)
        t_g = P.dma("sync", grow[0:1, 0:1024], din["l1_gamma"])
        t_b = P.dma("sync", grow[0:1, 1024:2048], din["l1_beta"])
        P.waits("scalar", [t_lr])
        t_sg = P.op("scalar", lambda e: e.activation(
            lam_sb[:, 0:KC], lam_sb[:, 0:KC], AF.Sigmoid), inc="ACT")
        P.waits("vector", [t_sg])
        t_oml = P.op("vector", lambda e: e.tensor_scalar(
            lam_sb[:, 8:8 + KC], lam_sb[:, 0:KC], -1.0, 1.0, ALU.mult,
            ALU.add), inc="DVE")
        for gi in range(4):
            P.waits("tensor", [t_g, t_b, cx.psD_free])
            t_gm = P.op("tensor", lambda e, gi=gi: e.matmul(
                cx.psD[:, 0:512], cx.ones_f32[0:1, :],
                grow[0:1, gi * 512:(gi + 1) * 512], start=True, stop=True),
                inc="PE")
            P.waits("vector", [t_gm])
            t_gc = P.op("vector", lambda e, gi=gi: e.tensor_copy(
                gb_sb[:, gi * 512:(gi + 1) * 512], cx.psD[:, 0:512]), inc="DVE")
            cx.psD_free = t_gc

        jw, wb, war_w = w_pool.acquire()
        wbv = wb.rearrange("p (k j) -> p k j", k=KC)
        P.waits("gpsimd", [war_w])
        t_wb = P.dma("gpsimd", wbv,
                     din["l1_WbT"].rearrange("(k p) j -> p k j", p=128))
        for oc in range(KC):
            jv, vst, war_v = vstage_pool.acquire()
            for tt in range(KC):
                jx, xnr, war_x = xnr_pool.acquire()
                xv = xnr.rearrange("p (k f) -> p k f", k=KC)
                P.waits("sync", [war_x])
                t_xn = P.dma("sync", xv,
                             cx.XN[:, :, tt * 512:(tt + 1) * 512].rearrange(
                                 "k p f -> p k f"))
                pst = cx.psC0 if tt % 2 == 0 else cx.psC1
                pfree = cx.psC0_free if tt % 2 == 0 else cx.psC1_free
                P.waits("tensor", [t_wb, t_xn, pfree])
                for k in range(KC):
                    t_mm = P.op("tensor", lambda e, k=k, oc=oc, pst=pst, xv=xv:
                                e.matmul(pst[:],
                                         wbv[:, k, oc * 128:(oc + 1) * 128],
                                         xv[:, k, :],
                                         start=(k == 0), stop=(k == KC - 1)),
                                inc="PE" if k == KC - 1 else None)
                xnr_pool.done(jx, t_mm)
                P.waits("vector", [t_mm, t_oml, t_bb] +
                        ([war_v] if tt == 0 else []))
                t_v = P.op("vector", lambda e, vst=vst, tt=tt, oc=oc, pst=pst:
                           e.tensor_scalar(vst[:, tt * 512:(tt + 1) * 512],
                                           pst[:], lam_sb[:, 16 + oc:17 + oc],
                                           lam_sb[:, 8 + oc:9 + oc],
                                           ALU.add, ALU.mult), inc="DVE")
                if tt % 2 == 0:
                    cx.psC0_free = t_v
                else:
                    cx.psC1_free = t_v
            if "dbgV" in DEV_FLAGS and oc == 0:
                P.waits("sync", [P.token("DVE")])
                P.dma("sync", cx.DBGV, vst[:])
            t_lb = P.op("vector", lambda e, oc=oc: e.tensor_scalar(
                lam_bc[:], gb_sb[:], 0.0, lam_sb[:, oc:oc + 1], ALU.mult,
                ALU.add), inc="DVE")
            for s in range(NSEQ):
                t_sc = P.op("vector", lambda e, oc=oc, s=s, vst=vst:
                            e.tensor_tensor_scan(
                                HHv[:, oc, s * 2048:(s + 1) * 2048],
                                lam_bc[:],
                                vst[:, s * 2048:(s + 1) * 2048],
                                0.0, ALU.mult, ALU.add), inc="DVE")
            vstage_pool.done(jv, t_sc)
        w_pool.done(jw, P.token("PE"))
        if "dbgH" in DEV_FLAGS:
            P.waits("sync", [P.token("DVE")])
            P.dma("sync", cx.DBGH, HH[:])

        jw, wc, war_w = w_pool.acquire()
        wcv = wc.rearrange("p (k j) -> p k j", k=KC)
        P.waits("gpsimd", [war_w])
        t_wc = P.dma("gpsimd", wcv,
                     din["l1_WcT"].rearrange("(k p) j -> p k j", p=128))
        for tt in range(KC):
            jv, zst, war_v = vstage_pool.acquire()
            zv = zst.rearrange("p (o f) -> p o f", o=KC)
            for oc in range(KC):
                pst = cx.psC0 if oc % 2 == 0 else cx.psC1
                pfree = cx.psC0_free if oc % 2 == 0 else cx.psC1_free
                P.waits("tensor", [t_wc, t_sc if oc == 0 and tt == 0 else None,
                                   pfree])
                for k in range(KC):
                    t_mm = P.op("tensor", lambda e, k=k, oc=oc, tt=tt, pst=pst:
                                e.matmul(pst[:],
                                         wcv[:, k, oc * 128:(oc + 1) * 128],
                                         HHv[:, k, tt * 512:(tt + 1) * 512],
                                         start=(k == 0), stop=(k == KC - 1)),
                                inc="PE" if k == KC - 1 else None)
                jxb, xnb, war_xb = xnb_pool.acquire()
                P.waits("sync", [war_xb])
                t_xb = P.dma("sync", xnb[:],
                             cx.XN[oc, :, tt * 512:(tt + 1) * 512])
                P.waits("vector", [t_mm, t_xb, t_bc] +
                        ([war_v] if oc == 0 else []))
                t_z = P.op("vector", lambda e, oc=oc, zv=zv, pst=pst, xnb=xnb:
                           e.scalar_tensor_tensor(
                               zv[:, oc, :], pst[:],
                               lam_sb[:, 24 + oc:25 + oc], xnb[:],
                               ALU.add, ALU.add), inc="DVE")
                xnb_pool.done(jxb, t_z)
                if oc % 2 == 0:
                    cx.psC0_free = t_z
                else:
                    cx.psC1_free = t_z
            for tc in range(4):
                P.waits("tensor", [t_z, cx.psA_frees[0], cx.psA_frees[1]])
                for oc in range(KC):
                    t_tr = P.op("tensor", lambda e, oc=oc, tc=tc, zv=zv:
                                e.transpose(
                                    cx.psA[:, oc * 128:(oc + 1) * 128],
                                    zv[:, oc, tc * 128:(tc + 1) * 128],
                                    cx.ident_f32[:]),
                                inc="PE" if oc == KC - 1 else None)
                js, stt, war_s = cx.stat_pool.acquire()
                jz, zn, war_z = zn_pool.acquire()
                P.waits("vector", [t_tr, war_s])
                t1 = P.op("vector", lambda e, stt=stt: e.reduce_sum(
                    stt[:, 4:5], cx.psA[:, 0:1024], axis=AX.X), inc="DVE")
                P.waits("scalar", [t_tr, P.token("DVE")])
                t2 = P.op("scalar", lambda e: e.activation(
                    zsq[:], cx.psA[:, 0:1024], AF.Square), inc="ACT")
                P.waits("vector", [t2])
                P.op("vector", lambda e, stt=stt: e.reduce_sum(
                    stt[:, 5:6], zsq[:], axis=AX.X))
                P.op("vector", lambda e, stt=stt: e.tensor_scalar(
                    stt[:, 0:1], stt[:, 4:5], -1.0 / D, None, ALU.mult))
                P.op("vector", lambda e, stt=stt: e.tensor_tensor(
                    stt[:, 6:7], stt[:, 0:1], stt[:, 0:1], ALU.mult))
                t3 = P.op("vector", lambda e, stt=stt: e.tensor_scalar(
                    stt[:, 7:8], stt[:, 5:6], 1.0 / D, stt[:, 6:7], ALU.mult,
                    ALU.subtract), inc="DVE")
                P.waits("scalar", [t3])
                t4 = P.op("scalar", lambda e, stt=stt: e.activation(
                    stt[:, 2:3], stt[:, 7:8], AF.Sqrt, bias=cx.cst[:, 0:1]),
                    inc="ACT")
                P.waits("vector", [t4, war_z])
                P.op("vector", lambda e, stt=stt: e.reciprocal(
                    stt[:, 1:2], stt[:, 2:3]))
                P.op("vector", lambda e, zn=zn, stt=stt: e.tensor_scalar(
                    zn[:], cx.psA[:, 0:1024], stt[:, 0:1], stt[:, 1:2],
                    ALU.add, ALU.mult))
                P.op("vector", lambda e, zn=zn: e.tensor_mul(
                    zn[:], zn[:], gb_sb[:, 0:1024]))
                t5 = P.op("vector", lambda e, zn=zn: e.tensor_tensor(
                    zn[:], zn[:], gb_sb[:, 1024:2048], ALU.add), inc="DVE")
                cx.psA_frees[0] = t5
                cx.psA_frees[1] = t5
                cx.stat_pool.done(js, t5)
                row0 = tt * 512 + tc * 128
                P.waits("sync", [t5])
                t_st = P.dma("sync", cx.DTM[row0:row0 + 128, :], zn[:])
                zn_pool.done(jz, t_st)
                toks.append(t_st)
            vstage_pool.done(jv, P.token("PE"))
        w_pool.done(jw, P.token("PE"))
    return toks


def emit_l3_ffn(P, cx, blk):
    from contextlib import ExitStack
    nc = cx.nc
    din = blk["din"]
    HC3 = 24
    P.barrier()
    toks = []
    with ExitStack() as stk:
        wd_i_pool = cx.mkpool(stk, 2, [128, 1024], I32)
        wd_bf = cx.mksb(stk, [128, HC3 * 1024], BF16)
        wdv = wd_bf.rearrange("p (h j) -> p h j", h=HC3)
        with ExitStack() as stk1:
            xn_sb = cx.mksb(stk1, [128, KC * NT], BF16)
            xnv = xn_sb.rearrange("p (k t) -> p k t", k=KC)
            wgu_i_pool = cx.mkpool(stk1, 2, [128, KC * 128], I32)
            wgu_pool = cx.mkpool(stk1, 2, [128, KC * 128], BF16)
            hrow_pool = cx.mkpool(stk1, 2, [128, NT], BF16)
            sg_pool = cx.mkpool(stk1, 2, [128, 512], F32)
            t_xn = P.dma("sync", xnv, cx.XN[:, :, :].rearrange("k p t -> p k t"))
            for hc in range(HC3):
                jwi, wdi, war_wi = wd_i_pool.acquire()
                P.waits("sync", [war_wi])
                t_wd = P.dma("sync", wdi[:],
                             din["l3_wdT"][hc * 128:(hc + 1) * 128, :])
                P.waits("vector", [t_wd])
                t_cv = P.op("vector", lambda e, hc=hc, wdi=wdi: e.tensor_copy(
                    wdv[:, hc, :], wdi[:]), inc="DVE")
                wd_i_pool.done(jwi, t_cv)
            for hc in range(HC3):
                jgi, wgi, war_gi = wgu_i_pool.acquire()
                jg, wg, war_g = wgu_pool.acquire()
                P.waits("sync", [war_gi])
                t_g = P.dma("sync", wgi.rearrange("p (k f) -> p k f", k=KC),
                            din["l3_wgT"][:, hc * 128:(hc + 1) * 128].rearrange(
                                "(k p) f -> p k f", p=128))
                P.waits("vector", [t_g, war_g])
                t_gc = P.op("vector", lambda e, wg=wg, wgi=wgi: e.tensor_copy(
                    wg[:], wgi[:]), inc="DVE")
                wgu_i_pool.done(jgi, t_gc)
                jui, wui, war_ui = wgu_i_pool.acquire()
                ju, wu, war_u = wgu_pool.acquire()
                P.waits("sync", [war_ui])
                t_u = P.dma("sync", wui.rearrange("p (k f) -> p k f", k=KC),
                            din["l3_wuT"][:, hc * 128:(hc + 1) * 128].rearrange(
                                "(k p) f -> p k f", p=128))
                P.waits("vector", [t_u, war_u])
                t_uc = P.op("vector", lambda e, wu=wu, wui=wui: e.tensor_copy(
                    wu[:], wui[:]), inc="DVE")
                wgu_i_pool.done(jui, t_uc)
                wgv = wg.rearrange("p (k f) -> p k f", k=KC)
                wuv = wu.rearrange("p (k f) -> p k f", k=KC)
                jh, hrow, war_h = hrow_pool.acquire()
                for tt in range(KC):
                    P.waits("tensor", [t_gc, t_xn, cx.psC0_free])
                    for k in range(KC):
                        t_m1 = P.op("tensor", lambda e, k=k, tt=tt, wgv=wgv:
                                    e.matmul(cx.psC0[:], wgv[:, k, :],
                                             xnv[:, k, tt * 512:(tt + 1) * 512],
                                             start=(k == 0), stop=(k == KC - 1)),
                                    inc="PE" if k == KC - 1 else None)
                    P.waits("tensor", [t_uc, cx.psC1_free])
                    for k in range(KC):
                        t_m2 = P.op("tensor", lambda e, k=k, tt=tt, wuv=wuv:
                                    e.matmul(cx.psC1[:], wuv[:, k, :],
                                             xnv[:, k, tt * 512:(tt + 1) * 512],
                                             start=(k == 0), stop=(k == KC - 1)),
                                    inc="PE" if k == KC - 1 else None)
                    jsg, sg, war_sg = sg_pool.acquire()
                    P.waits("scalar", [t_m1, war_sg])
                    t_si = P.op("scalar", lambda e, sg=sg: e.activation(
                        sg[:], cx.psC0[:], AF.Silu, scale=1.0 / 127.0),
                        inc="ACT")
                    cx.psC0_free = t_si
                    P.waits("vector", [t_si, t_m2] +
                            ([war_h] if tt == 0 else []))
                    t_h = P.op("vector", lambda e, tt=tt, sg=sg, hrow=hrow:
                               e.scalar_tensor_tensor(
                                   hrow[:, tt * 512:(tt + 1) * 512],
                                   cx.psC1[:], 1.0 / 127.0, sg[:],
                                   ALU.mult, ALU.mult), inc="DVE")
                    cx.psC1_free = t_h
                    sg_pool.done(jsg, t_h)
                wgu_pool.done(jg, P.token("PE"))
                wgu_pool.done(ju, P.token("PE"))
                P.waits("sync", [t_h])
                t_hs = P.dma("sync", blk["H3"][hc], hrow[:])
                hrow_pool.done(jh, t_hs)
        P.barrier()
        with ExitStack() as stk2:
            hg_pool = cx.mkpool(stk2, 2, [128, HC3 * 512], BF16)
            fst_pool = cx.mkpool(stk2, 3, [128, 512], F32)
            for tt in range(KC):
                jh2, hg, war_h2 = hg_pool.acquire()
                hgv = hg.rearrange("p (h f) -> p h f", h=HC3)
                P.waits("sync", [war_h2])
                t_hg = P.dma("sync", hgv,
                             blk["H3"][:, :, tt * 512:(tt + 1) * 512].rearrange(
                                 "h p f -> p h f"))
                for oc in range(KC):
                    pst = cx.psC0 if oc % 2 == 0 else cx.psC1
                    pfree = cx.psC0_free if oc % 2 == 0 else cx.psC1_free
                    P.waits("tensor", [t_hg, pfree])
                    for hc in range(HC3):
                        t_mm = P.op("tensor",
                                    lambda e, hc=hc, oc=oc, pst=pst, hgv=hgv:
                                    e.matmul(pst[:],
                                             wdv[:, hc, oc * 128:(oc + 1) * 128],
                                             hgv[:, hc, :],
                                             start=(hc == 0),
                                             stop=(hc == HC3 - 1)),
                                    inc="PE" if hc == HC3 - 1 else None)
                    jf, fst, war_f = fst_pool.acquire()
                    P.waits("vector", [t_mm, war_f])
                    t_ev = P.op("vector", lambda e, fst=fst, pst=pst:
                                e.tensor_scalar(fst[:], pst[:], 1.0 / 127.0,
                                                None, ALU.mult), inc="DVE")
                    if oc % 2 == 0:
                        cx.psC0_free = t_ev
                    else:
                        cx.psC1_free = t_ev
                    P.waits("gpsimd", [t_ev])
                    t_acc = P.dma("gpsimd",
                                  cx.DFM[oc, :, tt * 512:(tt + 1) * 512],
                                  fst[:], accum_op=ALU.add)
                    toks.append(t_acc)
                    fst_pool.done(jf, t_acc)
                hg_pool.done(jh2, P.token("PE"))
    return toks


def emit_residual(P, cx, blk):
    P.barrier()
    stk = blk["stk"]
    cx.racc_pool = cx.mkpool(stk, 2, [128, 1024], F32)
    cx.rtm_pool = cx.mkpool(stk, 2, [128, 1024], F32)
    for c in range(NCH):
        ja, acc, war_a = cx.racc_pool.acquire()
        P.waits("sync", [war_a] + blk["deps"])
        t_x = P.dma("sync", acc[:], blk["x"][c * 128:(c + 1) * 128, :])
        toks = [t_x]
        if blk["has_tm"]:
            jt, rtm, war_t = cx.rtm_pool.acquire()
            P.waits("sync", [war_t])
            t_tm = P.dma("sync", rtm[:], cx.DTM[c * 128:(c + 1) * 128, :])
            P.waits("vector", [t_x, t_tm])
            tok_a = P.op("vector", lambda e, acc=acc, rtm=rtm: e.tensor_tensor(
                acc[:], acc[:], rtm[:], ALU.add), inc="DVE")
            cx.rtm_pool.done(jt, tok_a)
            toks = [tok_a]
        if blk["has_fm"]:
            jt, rtm, war_t = cx.rtm_pool.acquire()
            P.waits("sync", [war_t])
            src = cx.DFM[:, :, c * 128:(c + 1) * 128].rearrange("k p f -> p k f")
            t_fm = P.dma("sync", rtm[:].rearrange("p (k f) -> p k f", k=KC), src)
            P.waits("tensor", [t_fm, cx.psA_frees[0], cx.psA_frees[1]])
            for k in range(KC):
                tok_tr = P.op("tensor", lambda e, rtm=rtm, k=k: e.transpose(
                    cx.psA[:, k * 128:(k + 1) * 128],
                    rtm[:, k * 128:(k + 1) * 128], cx.ident_f32[:]),
                    inc="PE" if k == KC - 1 else None)
            P.waits("vector", toks + [tok_tr])
            tok_a = P.op("vector", lambda e, acc=acc: e.tensor_tensor(
                acc[:], acc[:], cx.psA[:, 0:1024], ALU.add), inc="DVE")
            cx.psA_frees[0] = tok_a
            cx.psA_frees[1] = tok_a
            cx.rtm_pool.done(jt, tok_tr)
            toks = [tok_a]
        P.waits("sync", toks)
        t_st = P.dma("sync", blk["xout"][c * 128:(c + 1) * 128, :], acc[:])
        cx.racc_pool.done(ja, t_st)
    return P.token("dS")



SEMS = ("dS", "dG", "dC", "PE", "ACT", "DVE", "POOL", "mS", "mG", "mC")


def _alloc_common(stack, nc, cx, P):
    sb = lambda n, *a, **k: stack.enter_context(nc.sbuf_tensor("sb_" + n, *a, **k))
    ps = lambda n, *a, **k: stack.enter_context(nc.psum_tensor("ps_" + n, *a, **k))
    for s in SEMS:
        P.sem_handles[s] = stack.enter_context(nc.semaphore(s))
        P.semv.setdefault(s, 0)

    cx.psA = ps("psA", [128, 2048], F32)
    cx.psC0 = ps("psC0", [128, 512], F32)
    cx.psC1 = ps("psC1", [128, 512], F32)
    cx.psA_frees = [None, None, None, None]
    cx.psC0_free = None
    cx.psC1_free = None

    cx.ident_bf = sb("ident_bf", [128, 128], BF16)
    cx.ident_f32 = sb("ident_f32", [128, 128], F32)
    cx.ones_bf = sb("ones_bf", [1, 128], BF16)
    cx.ones_f32 = sb("ones_f32", [1, 128], F32)

    cx.stat_pool = Rot([sb(f"stat{i}", [128, 16], F32) for i in range(2)])
    cx.probs_pool = Rot([sb(f"prb{i}", [128, 96], F32) for i in range(2)])

    cx.cst = sb("cst", [128, 4], F32)
    cx.wr_sb = sb("wr_sb", [128, 128], BF16)
    cx.br_bc = sb("br_bc", [128, 16], F32)


def _load_consts(P, cx, din):
    t1 = P.dma("sync", cx.ident_bf[:], din["ident_bf"])
    t2 = P.dma("sync", cx.ident_f32[:], din["ident_f32"])
    t3 = P.dma("sync", cx.ones_bf[:], din["ones_row_bf"])
    t4 = P.dma("sync", cx.ones_f32[:], din["ones_row_f32"])
    t5 = P.op("vector", lambda e: e.memset(cx.cst[:, 0:1], EPS), inc="DVE")
    t6 = P.op("vector", lambda e: e.memset(cx.cst[:, 1:2], 0.0), inc="DVE")
    return [t1, t2, t3, t4, t5, t6]


def _router_cfg(P, cx, din, bi):
    if bi == 0:
        E, Er, kind, wr, br = 8, 8, "top2", din["l0_Wr"], None
    elif bi == 1:
        E, Er, kind, wr, br = 16, 15, "top3", din["l1_Wr"], din["l1_br"]
    elif bi == 2:
        E, Er, kind, wr, br = 8, 8, "top1inv", din["l2_Wr"], din["l2_br"]
    else:
        return None
    src = wr.rearrange("(k p) e -> p k e", p=128)
    tok_w = P.dma("gpsimd", cx.wr_sb[:, 0:KC * Er].rearrange(
        "p (k e) -> p k e", k=KC), src)
    toks = [tok_w]
    cfg = dict(E=E, Er=Er, kind=kind, wr_sb=cx.wr_sb, br_bc=None)
    if br is not None:
        jr, row, _ = cx.stat_pool.acquire()
        t_b = P.dma("sync", row[0:1, 0:Er], br)
        P.waits("tensor", [t_b, cx.psC0_free])
        tm = P.op("tensor", lambda e: e.matmul(
            cx.psC0[:, 0:Er], cx.ones_f32[0:1, :], row[0:1, 0:Er],
            start=True, stop=True), inc="PE")
        P.waits("vector", [tm])
        tc = P.op("vector", lambda e: e.tensor_copy(
            cx.br_bc[:, 0:Er], cx.psC0[:, 0:Er]), inc="DVE")
        cx.psC0_free = tc
        cx.stat_pool.done(jr, tc)
        cfg["br_bc"] = cx.br_bc
        toks.append(tc)
    cfg["ready"] = toks
    return cfg


def build_program(din_specs, blocks=(0, 1, 2, 3), dbg=(), phases=None):
    from contextlib import ExitStack
    if phases is None:
        phases = {"moe", "attn", "mix", "resid"}
    nc = bass.Bass("TRN2", target_bir_lowering=False, debug=False)
    cx = Ctx()
    cx.nc = nc
    P = Prog()
    cx.P = P

    din = {}
    for name, (shape, dt) in din_specs.items():
        din[name] = nc.dram_tensor(name, list(shape), dt, kind="ExternalInput").ap()

    def mk(name, shape, dt):
        kind = "ExternalOutput" if name in dbg else "Internal"
        return nc.dram_tensor(name, list(shape), dt, kind=kind).ap()

    y = nc.dram_tensor("y", [NT, D], F32, kind="ExternalOutput").ap()
    cx.XN = mk("XN", [KC, 128, NT], BF16)
    cx.DFM = mk("DFM", [KC, 128, NT], F32)
    cx.DTM = mk("DTM", [NT, D], F32)
    cx.GT = mk("GT", [16, NT], F32)
    cx.GTBF = mk("GTBF", [16, NT], BF16)
    cx.H0 = mk("H0", [8, 32, 128, NT], BF16)
    cx.H1 = mk("H1", [16, 8, 128, NT], BF16)
    cx.AO = mk("AO", [KC, 128, NT], BF16)
    cx.H3 = mk("H3", [24, 128, NT], BF16)
    if "DBGR" in dbg:
        cx.dbg_router = mk("DBGR", [128, 96], F32)
    cx.DBGH = mk("DBGH", [128, KC * NT], BF16)
    cx.DBGV = mk("DBGV", [128, NT], F32)
    XB = [mk(f"XB{i}", [NT, D], F32) for i in range(3)]

    moe_cfgs = {
        0: dict(E=8, HID=4096, W1="l0_W1", b1=None, W2="l0_W2", b2=None, H="H0"),
        1: dict(E=16, HID=1024, W1="l1_W1", b1="l1_b1", W2="l1_W2", b2="l1_b2",
                H="H1"),
        2: dict(E=8, HID=4096, W1="l2_W1", b1="l2_b1", W2="l2_W2", b2="l2_b2",
                H="H0"),
    }
    attn_cfgs = {
        0: dict(kind="l0", brow="b_qkv"),
        2: dict(kind="l2", brow=None),
        3: dict(kind="l3", brow="l3_b_attn"),
    }

    with ExitStack() as stack:
        _alloc_common(stack, nc, cx, P)
        ctoks = _load_consts(P, cx, din)
        prev = None
        xs = [din["x"]] + [XB[i] for i in range(3)]
        xs[blocks[0]] = din["x"]
        xouts = [XB[i] for i in range(3)] + [y]
        for bi in blocks:
            r = _router_cfg(P, cx, din, bi)
            if r is not None:
                P.waits("tensor", r["ready"] + ctoks)
            with ExitStack() as pstk:
                blk = dict(x=xs[bi], xn=cx.XN, router=r, prev_tok=prev, stk=pstk)
                emit_ln_router(P, cx, blk)
                if r is not None:
                    tok = P.token("DVE")
                    P.waits("sync", [tok])
                    P.dma("sync", cx.GT[0:r["E"], :], r["gatesT"][0:r["E"], :])
                    P.dma("sync", cx.GTBF[0:r["E"], :],
                          r["gatesT_bf"][0:r["E"], :])
            deps = []
            has_tm = False
            has_fm = False
            if bi in attn_cfgs and "attn" in phases:
                ac = attn_cfgs[bi]
                ablk = dict(bi=bi, kind=ac["kind"], din=din,
                            brow=din[ac["brow"]] if ac["brow"] else None,
                            m2tab=din.get("m2tab"), AO=cx.AO)
                deps += emit_attn(P, cx, ablk)
                has_fm = True
            if bi == 1 and "mix" in phases:
                deps += emit_l1_mixer(P, cx, dict(din=din))
                has_tm = True
            if bi in moe_cfgs and "moe" in phases:
                mc = moe_cfgs[bi]
                mblk = dict(bi=bi, E=mc["E"], HID=mc["HID"],
                            W1=din[mc["W1"]],
                            b1=din[mc["b1"]] if mc["b1"] else None,
                            W2=din[mc["W2"]],
                            b2=din[mc["b2"]] if mc["b2"] else None,
                            Hbuf=getattr(cx, mc["H"]),
                            accum_first_add=(bi == 1 and "mix" in phases))
                deps += emit_moe_dense(P, cx, mblk)
                has_tm = True
            if bi == 3 and "moe" in phases:
                deps += emit_l3_ffn(P, cx, dict(din=din, H3=cx.H3))
                has_fm = True
            if "resid" in phases:
                with ExitStack() as pstk:
                    xo = y if bi == max(blocks) else xouts[bi]
                    rblk = dict(x=xs[bi], xout=xo, has_tm=has_tm,
                                has_fm=has_fm, deps=deps, stk=pstk)
                    prev = emit_residual(P, cx, rblk)
            else:
                prev = P.token("dS")

        with nc.Block() as blk:
            @blk.sync
            def _(eng):
                for f in P.q["sync"]:
                    f(eng)
                eng.wait_ge(P.sem_handles["dS"], P.semv["dS"])

            @blk.scalar
            def _(eng):
                for f in P.q["scalar"]:
                    f(eng)

            @blk.tensor
            def _(eng):
                for f in P.q["tensor"]:
                    f(eng)

            @blk.vector
            def _(eng):
                for f in P.q["vector"]:
                    f(eng)

            @blk.gpsimd
            def _(eng):
                for f in P.q["gpsimd"]:
                    f(eng)
    return nc


DT_MAP = {"float32": F32, "int32": I32, "bfloat16": BF16}


def _din_specs(H, x_shape=(NT, D)):
    specs = {"x": (x_shape, F32), "m2tab": ((2, 128, 1024), F32)}
    for k, v in H.items():
        dt = DT_MAP[str(v.dtype)] if str(v.dtype) in DT_MAP else BF16
        specs[k] = (v.shape, dt)
    return specs


def kernel(**inputs):
    H = _prep_inputs(inputs)
    m2 = _l2_scale_tables()
    x = np.asarray(inputs["L0_attn_x"])
    specs = _din_specs(H)
    nc = build_program(specs)
    in_maps = []
    for c in range(NCORES):
        m = dict(H)
        m["x"] = np.ascontiguousarray(x[2 * c:2 * c + 2].reshape(NT, D))
        m["m2tab"] = np.ascontiguousarray(np.stack(
            [np.broadcast_to(m2[2 * c + s][None, :], (128, 1024))
             for s in range(2)]))
        in_maps.append(m)
    res = run_bass_kernel_spmd(nc, in_maps, core_ids=list(range(NCORES)))
    out = np.stack([r["y"] for r in res.results]).reshape(16, T, D)
    return out
